# version 12
# baseline (speedup 1.0000x reference)
import math
import numpy as np
import concourse.bass as bass
import concourse.mybir as mybir
from concourse.bass_utils import run_bass_kernel_spmd

F32 = mybir.dt.float32
BF16 = mybir.dt.bfloat16
AF = mybir.ActivationFunctionType
ALU = mybir.AluOpType
f32 = np.float32

B, S, D = 2, 4096, 4096
NH, NKV, HD = 32, 8, 128
NB = S // 128
LT = 1024
NC_D = D // 128
EXP_BIAS = -25.0
EXP_SCALE = 1.0 / math.sqrt(HD)

ENGINES = ("t", "a", "v", "g", "s")


class Plan:

    def __init__(self):
        self.ops = {e: [] for e in ENGINES}
        self.cnt = {e: 0 for e in ("TE", "AC", "DV", "LG", "LS", "ST", "XL")}

    def emit(self, eng, fn):
        self.ops[eng].append(fn)

    def bump(self, sem, amt):
        self.cnt[sem] += amt
        return self.cnt[sem]

    def _pre(self, e, sems, waits):
        for sname, val in waits:
            if val > 0:
                e.wait_ge(sems[sname], val)

    def mm(self, out, lhsT, rhs, start, stop, waits=(), inc=None):
        w = list(waits)
        val = self.bump(inc, 1) if inc else None

        def fn(e, sems, out=out, lhsT=lhsT, rhs=rhs, start=start, stop=stop,
               w=w, inc=inc):
            self._pre(e, sems, w)
            i = e.matmul(out, lhsT, rhs, start=start, stop=stop)
            if inc:
                i.then_inc(sems[inc], 1)
        self.emit("t", fn)
        return val

    def act(self, out, in_, func, bias=0.0, scale=1.0, waits=(), inc=None):
        w = list(waits)
        val = self.bump(inc, 1) if inc else None

        def fn(e, sems, out=out, in_=in_, func=func, bias=bias, scale=scale,
               w=w, inc=inc):
            self._pre(e, sems, w)
            i = e.activation(out, in_, func, bias=bias, scale=scale)
            if inc:
                i.then_inc(sems[inc], 1)
        self.emit("a", fn)
        return val

    def vec(self, name, args, waits=(), inc=None):
        w = list(waits)
        val = self.bump(inc, 1) if inc else None

        def fn(e, sems, name=name, args=args, w=w, inc=inc):
            self._pre(e, sems, w)
            i = getattr(e, name)(*args)
            if inc:
                i.then_inc(sems[inc], 1)
        self.emit("v", fn)
        return val

    def dma(self, q, out, in_, waits=(), sem=None):
        w = list(waits)
        sem = sem or ("LG" if q == "g" else "LS")
        val = self.bump(sem, 16)

        def fn(e, sems, out=out, in_=in_, w=w, sem=sem):
            self._pre(e, sems, w)
            e.dma_start(out=out, in_=in_).then_inc(sems[sem], 16)
        self.emit(q, fn)
        return val

    def wait(self, eng, waits):
        w = list(waits)

        def fn(e, sems, w=w):
            self._pre(e, sems, w)
        self.emit(eng, fn)

    def snap(self):
        return dict(self.cnt)

    def cut(self):
        if not hasattr(self, "cuts"):
            self.cuts = []
        self.cuts.append({e: len(self.ops[e]) for e in ENGINES})



def _wtile(src2d, ncols):
    return src2d.rearrange("(c p) o -> p c o", p=128)


def _wdst(dst2d, c, o):
    return dst2d.rearrange("p (c o) -> p c o", c=c)

def build(debug=False, phases=5, dbg_early=False):
    nc = bass.Bass("TRN2", target_bir_lowering=False)
    P = Plan()

    ap = lambda t: t.ap()
    xT = ap(nc.dram_tensor("xT", [D, S], F32, kind="ExternalInput"))
    xlT = ap(nc.dram_tensor("xlT", [D, LT], F32, kind="ExternalInput"))
    wkT = ap(nc.dram_tensor("wkT", [D, 1024], F32, kind="ExternalInput"))
    wvT = ap(nc.dram_tensor("wvT", [D, 1024], F32, kind="ExternalInput"))
    wqT = ap(nc.dram_tensor("wqT", [D, D], F32, kind="ExternalInput"))
    woT = ap(nc.dram_tensor("woT", [D, D], F32, kind="ExternalInput"))
    wT = ap(nc.dram_tensor("wT", [D, D], F32, kind="ExternalInput"))
    aallT = ap(nc.dram_tensor("aallT", [D, 128], F32, kind="ExternalInput"))
    ballT = ap(nc.dram_tensor("ballT", [128, D], F32, kind="ExternalInput"))
    bias2d = ap(nc.dram_tensor("bias2d", [128, 32], F32, kind="ExternalInput"))
    w2rep = ap(nc.dram_tensor("w2rep", [128, 1], F32, kind="ExternalInput"))
    cosk = ap(nc.dram_tensor("cosk", [128, S], F32, kind="ExternalInput"))
    ssgnk = ap(nc.dram_tensor("ssgnk", [128, S], F32, kind="ExternalInput"))
    cosq = ap(nc.dram_tensor("cosq", [128, LT], F32, kind="ExternalInput"))
    ssgnq = ap(nc.dram_tensor("ssgnq", [128, LT], F32, kind="ExternalInput"))
    maskin = ap(nc.dram_tensor("maskin", [128, 16 * 512], F32,
                               kind="ExternalInput"))
    y = ap(nc.dram_tensor("y", [NC_D, 128, LT], F32, kind="ExternalOutput"))

    scr_kind = "ExternalOutput" if debug else "Internal"
    k_scr = ap(nc.dram_tensor("k_scr", [8, 128, S], BF16, kind=scr_kind))
    v_scr = ap(nc.dram_tensor("v_scr", [NB, 128, 1024], BF16, kind=scr_kind))
    q_scr = ap(nc.dram_tensor("q_scr", [NH, 128, LT], BF16, kind=scr_kind))
    at_scr = ap(nc.dram_tensor("at_scr", [NH, 128, LT], BF16, kind=scr_kind))
    dbg_x1 = dbg_s2 = dbg_bk = None
    if debug:
        dbg_x1 = ap(nc.dram_tensor("dbg_x1", [128, 32768], BF16,
                                   kind="ExternalOutput"))
        dbg_s2 = ap(nc.dram_tensor("dbg_s2", [128, 32768], BF16,
                                   kind="ExternalOutput"))
        dbg_bk = ap(nc.dram_tensor("dbg_bk", [128, 2048], BF16,
                                   kind="ExternalOutput"))

    ABF_N = 90112
    AF32_N = 6656

    OFF = dict(
        wk=0, wv=32768, xb0=65536, xb1=65536 + 8192,

        xl=0, wqb0=32768, wqb1=32768 + 4096, cosq=40960, ssgnq=40960 + 1024,
        V=0, pt=32768, K=40960, mask=73728, qbA0=81920,
        qbB0=82944, qbA1=83968, qbB1=84992, pt_unused=0,
        attn=0, S1=32768, wob0=65536, wob1=65536 + 4096, sq0=73728,
        sq1=73728 + 512,
        S2=0, ab=73728, bb=77824, sq5_0=81920, sq5_1=81920 + 512, hsc=82944,
        recb=84992,
        stg=86016, ones=88064, onesm=88192, praw0=38912, praw1=39936,
    )
    F = dict(xlres0=0, xlres1=1024, ostg0=2048, ostg1=3072, rtmp0=4096,
             rtmp1=4608, bias=5120, w2=5152, onesk=5153, dsb=5281, rec=5793,
             ms=5281, sqs=4096,
             coskt0=2048, coskt1=2304, ssgnkt0=2560, ssgnkt1=2816,
             cosq=0, ssgnq=1024)

    with (
        nc.sbuf_tensor("abf", [128, ABF_N], BF16) as abf_t,
        nc.sbuf_tensor("af32", [128, AF32_N], F32) as af32_t,
        nc.psum_tensor("psum", [128, 4096], F32) as psum_t,
        nc.semaphore("TE") as sTE,
        nc.semaphore("AC") as sAC,
        nc.semaphore("DV") as sDV,
        nc.semaphore("LG") as sLG,
        nc.semaphore("LS") as sLS,
        nc.semaphore("ST") as sST,
        nc.semaphore("XL") as sXL,
        nc.Block() as block,
    ):
        A = abf_t[:]
        AF_ = af32_t[:]
        PS = psum_t[:]
        sems = {"TE": sTE, "AC": sAC, "DV": sDV, "LG": sLG, "LS": sLS,
                "ST": sST, "XL": sXL}

        def bank(i, p0=0, p1=128, n=512):
            return PS[p0:p1, i * 512:i * 512 + n]

        def bf(off, n, p0=0, p1=128):
            return A[p0:p1, off:off + n]

        def ff(off, n, p0=0, p1=128):
            return AF_[p0:p1, off:off + n]

        stg_ls = [0, 0, 0, 0]
        stg_i = [0]

        def stage(dv_val, scr_ap, width):
            s = stg_i[0] % 4
            stg_i[0] += 1
            v = P.dma("s", scr_ap, bf(OFF["stg"] + s * 512, width),
                      waits=[("DV", dv_val)])
            stg_ls[s] = v

        def next_stage():
            s = stg_i[0] % 4
            return s, stg_ls[s]

        P.dma("g", ff(F["bias"], 32), bias2d[:, :])
        P.dma("g", ff(F["w2"], 1), w2rep[:, :])
        c_w2 = P.cnt["LG"]
        P.vec("tensor_scalar_mul", (ff(F["w2"], 1), ff(F["w2"], 1), 2.0),
              waits=[("LG", c_w2)], inc="DV")
        P.vec("memset", (bf(OFF["ones"], 1), 1.0), inc="DV")
        P.vec("memset", (bf(OFF["onesm"], 128), 1.0), inc="DV")
        P.vec("memset", (ff(F["onesk"], 128, 0, 1), 1.0), inc="DV")

        c_wk = P.dma("g", _wdst(bf(OFF["wk"], 32768), 32, 1024),
                     wkT.rearrange("(c p) o -> p c o", p=128))
        c_pre = P.dma("g", _wdst(bf(OFF["wv"], 32768), 32, 1024),
                      wvT.rearrange("(c p) o -> p c o", p=128))
        wk_part = [c_wk] * 4
        wv_part = [c_pre] * 4

        NT1 = 16
        xload = [0] * NT1
        cload = [0] * NT1
        kv_te_done = [0] * NT1
        dv_done = {}

        def p1_load(i):
            w = [("LG", c_pre)] if i < 2 else [("TE", kv_te_done[i - 2])]
            xb = OFF["xb0"] if i % 2 == 0 else OFF["xb1"]
            P.dma("g", _wdst(bf(xb, 8192), 32, 256),
                  xT[:, i * 256:(i + 1) * 256].rearrange(
                      "(c p) t -> p c t", p=128), waits=w)
            ck = F["coskt0"] if i % 2 == 0 else F["coskt1"]
            sk = F["ssgnkt0"] if i % 2 == 0 else F["ssgnkt1"]
            P.dma("g", ff(ck, 256), cosk[:, i * 256:(i + 1) * 256])
            cload[i] = P.dma("g", ff(sk, 256), ssgnk[:, i * 256:(i + 1) * 256])
            xload[i] = cload[i]

        p1_load(0)
        p1_load(1)
        kvmm = 0
        for i in range(NT1):
            xb = OFF["xb0"] if i % 2 == 0 else OFF["xb1"]
            ck = F["coskt0"] if i % 2 == 0 else F["coskt1"]
            sk = F["ssgnkt0"] if i % 2 == 0 else F["ssgnkt1"]
            for oh in range(8):
                bk = kvmm % 2
                kvmm += 1
                w = [("LG", max(xload[i], wk_part[oh // 2]))]
                key = ("kb", bk)
                if key in dv_done:
                    w.append(("DV", dv_done[key]))
                for dc in range(32):
                    te = P.mm(bank(bk, n=256),
                              bf(OFF["wk"] + dc * 1024 + oh * 128, 128),
                              bf(xb + dc * 256, 256),
                              start=(dc == 0), stop=(dc == 31),
                              waits=(w if dc == 0 else ()),
                              inc=("TE" if dc == 31 else None))
                ssl, need_ls = next_stage()
                wv_ = [("TE", te), ("LS", need_ls)]
                P.vec("tensor_mul", (ff(F["rtmp0"], 256, 0, 64),
                                     bank(bk, 64, 128, 256),
                                     ff(sk, 256, 0, 64)), waits=wv_)
                P.vec("tensor_mul", (ff(F["rtmp0"], 256, 64, 128),
                                     bank(bk, 0, 64, 256),
                                     ff(sk, 256, 64, 128)))
                P.vec("tensor_mul", (ff(F["rtmp1"], 256), bank(bk, n=256),
                                     ff(ck, 256)))
                dvv = P.vec("tensor_add", (bf(OFF["stg"] + ssl * 512, 256),
                                           ff(F["rtmp1"], 256),
                                           ff(F["rtmp0"], 256)), inc="DV")
                dv_done[("kb", bk)] = dvv
                stage(dvv, k_scr[oh, :, i * 256:i * 256 + 256], 256)
            for tb in range(2):
                for ov in range(2):
                    bv = 2 + kvmm % 2
                    kvmm += 1
                    w = [("LG", max(xload[i], wv_part[2 * ov + 1]))]
                    key = ("vb", bv)
                    if key in dv_done:
                        w.append(("DV", dv_done[key]))
                    for dc in range(32):
                        te = P.mm(bank(bv),
                                  bf(xb + dc * 256 + tb * 128, 128),
                                  bf(OFF["wv"] + dc * 1024 + ov * 512, 512),
                                  start=(dc == 0), stop=(dc == 31),
                                  waits=(w if dc == 0 else ()),
                                  inc=("TE" if dc == 31 else None))
                    ssl, need_ls = next_stage()
                    dvv = P.vec("tensor_copy",
                                (bf(OFF["stg"] + ssl * 512, 512), bank(bv)),
                                waits=[("TE", te), ("LS", need_ls)], inc="DV")
                    dv_done[("vb", bv)] = dvv
                    gb = i * 2 + tb
                    stage(dvv, v_scr[gb, :, ov * 512:ov * 512 + 512], 512)
            kv_te_done[i] = P.cnt["TE"]
            if i + 2 < NT1:
                p1_load(i + 2)

        bar1 = P.snap()
        P.cut()
        c_xl = P.dma("g", _wdst(bf(OFF["xl"], 32768), 32, 1024),
                     xlT.rearrange("(c p) t -> p c t", p=128),
                     waits=[("TE", bar1["TE"]), ("DV", bar1["DV"])],
                     sem="XL")
        xl_part = [c_xl] * 8
        P.dma("g", ff(F["cosq"], 1024), cosq[:, :])
        c_q = P.dma("g", ff(F["ssgnq"], 1024), ssgnq[:, :])

        wq_ls = [0, 0]
        q_dv = {}
        q_te_done = {}
        q_st_ls = {}
        for oh in range(NH):
            buf = OFF["wqb0"] if oh % 2 == 0 else OFF["wqb1"]
            w = [("LG", c_q)] if oh < 2 else [("TE", q_te_done[oh - 2])]
            wq_ls[oh % 2] = P.dma(
                "g", _wdst(bf(buf, 4096), 32, 128),
                wqT[:, oh * 128:(oh + 1) * 128].rearrange(
                    "(c p) o -> p c o", p=128), waits=w)
            for T in range(2):
                bk = T
                wmm = [("LG", wq_ls[oh % 2])]
                key = ("qb", bk)
                if key in q_dv:
                    wmm.append(("DV", q_dv[key]))
                for dc in range(32):
                    wdc = list(wmm) if dc == 0 else []
                    if oh == 0 and T == 0 and dc % 4 == 0:
                        wdc.append(("XL", xl_part[dc // 4]))
                    te = P.mm(bank(bk), bf(buf + dc * 128, 128),
                              bf(OFF["xl"] + dc * 1024 + T * 512, 512),
                              start=(dc == 0), stop=(dc == 31),
                              waits=wdc,
                              inc=("TE" if dc == 31 else None))
                ssl, need_ls = next_stage()
                wv_ = [("TE", te), ("LS", need_ls)]
                P.vec("tensor_mul", (ff(F["rtmp0"], 512, 0, 64),
                                     bank(bk, 64, 128),
                                     ff(F["ssgnq"] + T * 512, 512, 0, 64)),
                      waits=wv_)
                P.vec("tensor_mul", (ff(F["rtmp0"], 512, 64, 128),
                                     bank(bk, 0, 64),
                                     ff(F["ssgnq"] + T * 512, 512, 64, 128)))
                P.vec("tensor_mul", (ff(F["rtmp1"], 512), bank(bk),
                                     ff(F["cosq"] + T * 512, 512)))
                dvv = P.vec("tensor_add", (bf(OFF["stg"] + ssl * 512, 512),
                                           ff(F["rtmp1"], 512),
                                           ff(F["rtmp0"], 512)), inc="DV")
                q_dv[("qb", bk)] = dvv
                stage(dvv, q_scr[oh, :, T * 512:T * 512 + 512], 512)
            q_st_ls[oh] = P.cnt["LS"]
            q_te_done[oh] = P.cnt["TE"]

        if debug and dbg_early:
            fe = P.snap()
            P.dma("s", dbg_s2, bf(OFF["xl"], 32768),
                  waits=[("TE", fe["TE"]), ("DV", fe["DV"]), ("LG", fe["LG"])])
            P.wait("s", [("LS", P.cnt["LS"])])
        bar2 = P.snap()
        P.cut()
        wbar = [("TE", bar2["TE"]), ("DV", bar2["DV"])]
        wbar.append(("LS", bar2["LS"]))
        P.dma("s", _wdst(bf(OFF["K"], 32768), 8, 4096),
              k_scr.rearrange("h p t -> p h t"), waits=wbar)
        c_kv = P.dma("s", _wdst(bf(OFF["V"], 32768), 32, 1024),
                     v_scr.rearrange("b p o -> p b o"))
        v_part = [c_kv] * 4
        c_mask = P.dma("g", bf(OFF["mask"], 8192), maskin[:, :],
                       waits=[("TE", bar2["TE"]), ("DV", bar2["DV"])])

        LAG = 3
        NPT = 6
        q_ls = [0, 0]
        m = 0
        ac_exp = {}
        te_sb = {}
        te_pvp = {}
        p_ready = {}
        pair_last_te = {}
        tail_evac = {}
        tail_recip = {}
        tail_acp = {}
        stg3_ls = [0, 0]
        gT = 0
        def q_prefetch(gg):
            qa_ = OFF["qbA0"] if gg % 2 == 0 else OFF["qbA1"]
            qbb = OFF["qbB0"] if gg % 2 == 0 else OFF["qbB1"]
            wq_ = [("LS", max(q_st_ls[2 * gg + 1], c_kv))]
            if gg >= 2:
                wq_.append(("TE", pair_last_te[gg - 2]))
            P.dma("s", bf(qa_, 1024), q_scr[2 * gg, :, :], waits=wq_)
            q_ls[gg % 2] = P.dma("s", bf(qbb, 1024), q_scr[2 * gg + 1, :, :])

        q_prefetch(0)
        for g in range(16):
            hA, hB = 2 * g, 2 * g + 1
            kvh = g // 2
            qa = OFF["qbA0"] if g % 2 == 0 else OFF["qbA1"]
            qb_ = OFF["qbB0"] if g % 2 == 0 else OFF["qbB1"]
            if g + 1 < 16:
                q_prefetch(g + 1)
            for T in range(2):
                nch = 16 * (T + 1)
                mbase = m
                for c in range(nch + LAG):
                    if c < nch:
                        mm_ = m
                        m += 1
                        sb2 = 2 * (mm_ % 2)
                        wsc = [("LS", q_ls[g % 2])]
                        if mm_ >= 2:
                            wsc.append(("AC", ac_exp[mm_ - 2]))
                        P.mm(bank(sb2),
                             bf(OFF["K"] + kvh * 4096 + c * 128, 128),
                             bf(qa + T * 512, 512), start=True, stop=True,
                             waits=wsc)
                        te_sb[mm_] = P.mm(
                            bank(sb2 + 1),
                            bf(OFF["K"] + kvh * 4096 + c * 128, 128),
                            bf(qb_ + T * 512, 512), start=True, stop=True,
                            inc="TE")
                        pt = OFF["pt"] + (mm_ % NPT) * 1024
                        wex = [("TE", te_sb[mm_])]
                        if mm_ >= NPT:
                            wex.append(("TE", te_pvp[mm_ - NPT]))
                        ac_exp[mm_] = P.act(
                            bf(pt, 1024), PS[:, sb2 * 512:sb2 * 512 + 1024],
                            AF.Exp, bias=f32(EXP_BIAS), scale=f32(EXP_SCALE),
                            waits=wex, inc="AC")
                        if c >= 16 * T:
                            rel = c - 16 * T
                            P.vec("tensor_mul",
                                  (bf(pt, 512), bf(pt, 512),
                                   bf(OFF["mask"] + rel * 512, 512)),
                                  waits=[("AC", ac_exp[mm_]), ("LG", c_mask)])
                            dvm = P.vec(
                                "tensor_mul",
                                (bf(pt + 512, 512), bf(pt + 512, 512),
                                 bf(OFF["mask"] + rel * 512, 512)), inc="DV")
                            p_ready[mm_] = ("DV", dvm)
                        else:
                            p_ready[mm_] = ("AC", ac_exp[mm_])
                    if c >= LAG:
                        cc = c - LAG
                        mcc = mbase + cc
                        ptc = OFF["pt"] + (mcc % NPT) * 1024
                        wpv = [p_ready[mcc], ("LS", v_part[cc // 8])]
                        if cc == 0 and gT >= 1:
                            wpv.append(("AC", tail_acp[gT - 1]))
                        P.mm(bank(4),
                             bf(OFF["V"] + cc * 1024 + kvh * 128, 128),
                             bf(ptc, 512), start=(cc == 0),
                             stop=(cc == nch - 1), waits=wpv)
                        wdn = []
                        if cc == 0 and gT >= 1:
                            wdn.append(("DV", tail_recip[gT - 1]))
                        P.mm(bank(6), bf(OFF["onesm"], 128),
                             bf(ptc, 512), start=(cc == 0),
                             stop=(cc == nch - 1), waits=wdn)
                        P.mm(bank(5),
                             bf(OFF["V"] + cc * 1024 + kvh * 128, 128),
                             bf(ptc + 512, 512), start=(cc == 0),
                             stop=(cc == nch - 1))
                        te_pvp[mcc] = P.mm(
                            bank(7), bf(OFF["onesm"], 128),
                            bf(ptc + 512, 512), start=(cc == 0),
                            stop=(cc == nch - 1), inc="TE")
                ssl = gT % 2
                rb = 0 if gT % 2 == 0 else 1024
                praw = OFF["praw0"] if gT % 2 == 0 else OFF["praw1"]
                wacp = [("TE", te_pvp[mbase + nch - 1])]
                if gT >= 2:
                    wacp.append(("DV", tail_evac[gT - 2]))
                tail_acp[gT] = P.act(bf(praw, 1024), PS[:, 2048:3072],
                                     AF.Copy, waits=wacp, inc="AC")
                P.vec("reciprocal", (ff(rb, 512), bank(6)),
                      waits=[("TE", te_pvp[mbase + nch - 1])])
                tail_recip[gT] = P.vec("reciprocal",
                                       (ff(rb + 512, 512), bank(7)), inc="DV")
                wmu = [("AC", tail_acp[gT]), ("LS", stg3_ls[ssl])]
                P.vec("tensor_mul", (bf(OFF["stg"] + ssl * 1024, 512),
                                     bf(praw, 512), ff(rb, 512)), waits=wmu)
                tail_evac[gT] = P.vec(
                    "tensor_mul", (bf(OFF["stg"] + ssl * 1024 + 512, 512),
                                   bf(praw + 512, 512), ff(rb + 512, 512)),
                    inc="DV")
                P.dma("s", at_scr[hA, :, T * 512:T * 512 + 512],
                      bf(OFF["stg"] + ssl * 1024, 512),
                      waits=[("DV", tail_evac[gT])])
                stg3_ls[ssl] = P.dma(
                    "s", at_scr[hB, :, T * 512:T * 512 + 512],
                    bf(OFF["stg"] + ssl * 1024 + 512, 512))
                gT += 1
                P.cut()
            pair_last_te[g] = P.cnt["TE"]

        bar3 = P.snap()
        P.cut()
        c_attn = P.dma("s", _wdst(bf(OFF["attn"], 32768), 32, 1024),
                       at_scr.rearrange("h p t -> p h t"),
                       waits=[("TE", bar3["TE"]), ("DV", bar3["DV"]),
                              ("AC", bar3["AC"]), ("LS", bar3["LS"])])
        attn_part = [c_attn] * 8

        wo_banks = {0: (0, 1), 1: (6, 7)}
        wo_ls = [0, 0]
        xl_ls = [0, 0]
        dv_s1 = {}
        te_sq = {}
        te_wo_done = {}
        for oc in range(NC_D):
            bufw = OFF["wob0"] if oc % 2 == 0 else OFF["wob1"]
            w = ([("DV", bar3["DV"]), ("TE", bar3["TE"])] if oc < 2
                 else [("TE", te_wo_done[oc - 2])])
            wo_ls[oc % 2] = P.dma(
                "g", _wdst(bf(bufw, 4096), 32, 128),
                woT[:, oc * 128:(oc + 1) * 128].rearrange(
                    "(c p) o -> p c o", p=128), waits=w)
            wx = [("DV", dv_s1[oc - 2])] if oc >= 2 else []
            xl_ls[oc % 2] = P.dma(
                "s", ff(F["xlres0"] if oc % 2 == 0 else F["xlres1"], 1024),
                xlT[oc * 128:(oc + 1) * 128, :], waits=wx)
            for T in range(2):
                bk = wo_banks[oc % 2][T]
                wmm = [("LG", wo_ls[oc % 2])]
                if oc >= 2:
                    wmm.append(("DV", dv_s1[oc - 2]))
                for dc in range(32):
                    wdc = list(wmm) if dc == 0 else []
                    if oc == 0 and T == 0 and dc % 4 == 0:
                        wdc.append(("LS", attn_part[dc // 4]))
                    te = P.mm(bank(bk), bf(bufw + dc * 128, 128),
                              bf(OFF["attn"] + dc * 1024 + T * 512, 512),
                              start=(dc == 0), stop=(dc == 31),
                              waits=wdc,
                              inc=("TE" if dc == 31 else None))
                s1c = OFF["S1"] + oc * 1024 + T * 512
                dvv = P.vec("tensor_add",
                            (bf(s1c, 512), bank(bk),
                             ff((F["xlres0"] if oc % 2 == 0 else F["xlres1"])
                                + T * 512, 512)),
                            waits=[("TE", te), ("LS", xl_ls[oc % 2])],
                            inc="DV")
                P.mm(bank(2 + T, 0, 1), bf(OFF["ones"], 1), bf(s1c, 512),
                     start=(oc == 0), stop=(oc == NC_D - 1),
                     waits=[("DV", dvv)])
                sqb = OFF["sq0"] if T == 0 else OFF["sq1"]
                wsq = [("DV", dvv)]
                if oc >= 1:
                    wsq.append(("TE", te_sq[(oc - 1, T)]))
                acs = P.act(bf(sqb, 512), bf(s1c, 512), AF.Square,
                            waits=wsq, inc="AC")
                te_sq[(oc, T)] = P.mm(
                    bank(4 + T, 0, 1), bf(OFF["ones"], 1), bf(sqb, 512),
                    start=(oc == 0), stop=(oc == NC_D - 1),
                    waits=[("AC", acs)], inc="TE")
                if T == 1:
                    dv_s1[oc] = P.cnt["DV"]
            te_wo_done[oc] = P.cnt["TE"]

        def ln_finalize(norm_src_off, out_f32):
            te_all = P.cnt["TE"]
            ac1 = P.act(ff(F["ms"], 512, 0, 1), bank(2, 0, 1), AF.Copy,
                        scale=f32(1.0 / D), waits=[("TE", te_all)], inc="AC")
            P.act(ff(F["ms"] + 512, 512, 0, 1), bank(3, 0, 1), AF.Copy,
                  scale=f32(1.0 / D))
            P.act(ff(F["sqs"], 512, 0, 1), bank(4, 0, 1), AF.Copy,
                  scale=f32(1.0 / D))
            ac2 = P.act(ff(F["sqs"] + 512, 512, 0, 1), bank(5, 0, 1), AF.Copy,
                        scale=f32(1.0 / D), inc="AC")
            dv1 = P.vec("tensor_mul", (ff(F["dsb"] + 0, 1024, 0, 1),
                                       ff(F["ms"], 1024, 0, 1),
                                       ff(F["ms"], 1024, 0, 1)),
                        waits=[("AC", ac2)], inc="DV")
            return ac1, dv1

        M2 = 2048
        te_all = P.cnt["TE"]
        P.act(ff(F["ms"], 512, 0, 1), bank(2, 0, 1), AF.Copy,
              scale=f32(1.0 / D), waits=[("TE", te_all)])
        P.act(ff(F["ms"] + 512, 512, 0, 1), bank(3, 0, 1), AF.Copy,
              scale=f32(1.0 / D))
        P.act(ff(F["sqs"], 512, 0, 1), bank(4, 0, 1), AF.Copy,
              scale=f32(1.0 / D))
        ac2 = P.act(ff(F["sqs"] + 512, 512, 0, 1), bank(5, 0, 1), AF.Copy,
                    scale=f32(1.0 / D), inc="AC")
        P.vec("tensor_mul", (ff(M2, 1024, 0, 1), ff(F["ms"], 1024, 0, 1),
                             ff(F["ms"], 1024, 0, 1)), waits=[("AC", ac2)])
        dv2 = P.vec("tensor_sub", (ff(F["sqs"], 1024, 0, 1),
                                   ff(F["sqs"], 1024, 0, 1),
                                   ff(M2, 1024, 0, 1)), inc="DV")
        ac3 = P.act(ff(F["sqs"], 1024, 0, 1), ff(F["sqs"], 1024, 0, 1),
                    AF.Sqrt, bias=f32(1e-5), waits=[("DV", dv2)], inc="AC")
        P.vec("reciprocal", (ff(F["sqs"], 1024, 0, 1),
                             ff(F["sqs"], 1024, 0, 1)),
              waits=[("AC", ac3)])
        P.vec("tensor_mul", (ff(F["ms"], 1024, 0, 1), ff(F["ms"], 1024, 0, 1),
                             ff(F["sqs"], 1024, 0, 1)))
        dv5 = P.vec("tensor_scalar_mul", (ff(F["ms"], 1024, 0, 1),
                                          ff(F["ms"], 1024, 0, 1), -1.0),
                    inc="DV")
        P.mm(bank(0), ff(F["onesk"], 128, 0, 1), ff(F["sqs"], 512, 0, 1),
             start=True, stop=True, waits=[("DV", dv5)])
        P.mm(bank(1), ff(F["onesk"], 128, 0, 1),
             ff(F["sqs"] + 512, 512, 0, 1), start=True, stop=True)
        P.mm(bank(6), ff(F["onesk"], 128, 0, 1), ff(F["ms"], 512, 0, 1),
             start=True, stop=True)
        teb1 = P.mm(bank(7), ff(F["onesk"], 128, 0, 1),
                    ff(F["ms"] + 512, 512, 0, 1), start=True, stop=True,
                    inc="TE")
        dv_norm_oc = {}
        for T in range(2):
            for oc in range(NC_D):
                s1c = OFF["S1"] + oc * 1024 + T * 512
                P.vec("tensor_mul", (bf(s1c, 512), bf(s1c, 512), bank(T)),
                      waits=([("TE", teb1)] if (oc == 0 and T == 0) else ()))
                dv_norm_oc[(T, oc)] = P.vec(
                    "tensor_add", (bf(s1c, 512), bf(s1c, 512),
                                   bank(6 + T)), inc="DV")
        dv_norm = dv_norm_oc[(1, NC_D - 1)]

        bar4 = P.snap()
        P.cut()
        P.dma("g", _wdst(bf(OFF["ab"], 4096), 32, 128),
              aallT.rearrange("(c p) o -> p c o", p=128),
              waits=[("TE", bar4["TE"]), ("DV", bar4["DV"]),
                     ("AC", bar4["AC"])])
        c_lora = P.dma("g", bf(OFF["bb"], 4096), ballT[:, :])

        hsc_ac = 0
        for T in range(2):
            for dc in range(32):
                wdc = [("DV", dv_norm_oc[(T, dc)])]
                if dc == 0:
                    wdc.append(("LG", c_lora))
                te = P.mm(bank(2 + T), bf(OFF["ab"] + dc * 128, 128),
                          bf(OFF["S1"] + dc * 1024 + T * 512, 512),
                          start=(dc == 0), stop=(dc == 31),
                          waits=wdc,
                          inc=("TE" if dc == 31 else None))
            hsc_ac = P.act(bf(OFF["hsc"] + T * 512, 512), bank(2 + T),
                           AF.Copy, scale=ff(F["w2"], 1),
                           waits=[("TE", te)], inc="AC")

        w_ls = [0, 0]
        dv_s2 = {}
        te_sq2 = {}
        te_w_done = {}
        for oc in range(NC_D):
            bufw = OFF["wob0"] if oc % 2 == 0 else OFF["wob1"]
            w = ([("AC", hsc_ac)] if oc < 2 else
                 [("TE", te_w_done[oc - 2])])
            w_ls[oc % 2] = P.dma(
                "g", _wdst(bf(bufw, 4096), 32, 128),
                wT[:, oc * 128:(oc + 1) * 128].rearrange(
                    "(c p) o -> p c o", p=128), waits=w)
            for T in range(2):
                bk = wo_banks[oc % 2][T]
                wmm = [("LG", w_ls[oc % 2]), ("AC", hsc_ac)]
                if oc >= 2:
                    wmm.append(("DV", dv_s2[oc - 2]))
                if oc < 2:
                    wmm.append(("DV", dv_norm))
                for dc in range(32):
                    P.mm(bank(bk), bf(bufw + dc * 128, 128),
                         bf(OFF["S1"] + dc * 1024 + T * 512, 512),
                         start=(dc == 0), stop=False,
                         waits=(wmm if dc == 0 else ()))
                te = P.mm(bank(bk), bf(OFF["bb"] + oc * 128, 128),
                          bf(OFF["hsc"] + T * 512, 512),
                          start=False, stop=True, inc="TE")
                s2c = OFF["S2"] + oc * 1024 + T * 512
                dvv = P.vec("scalar_tensor_tensor",
                            (bf(s2c, 512), bank(bk), ff(F["bias"] + oc, 1),
                             bf(OFF["S1"] + oc * 1024 + T * 512, 512),
                             ALU.add, ALU.add),
                            waits=[("TE", te)], inc="DV")
                P.mm(bank(2 + T, 0, 1), bf(OFF["ones"], 1), bf(s2c, 512),
                     start=(oc == 0), stop=(oc == NC_D - 1),
                     waits=[("DV", dvv)])
                sqb = OFF["sq5_0"] if T == 0 else OFF["sq5_1"]
                wsq = [("DV", dvv)]
                if oc >= 1:
                    wsq.append(("TE", te_sq2[(oc - 1, T)]))
                acs = P.act(bf(sqb, 512), bf(s2c, 512), AF.Square,
                            waits=wsq, inc="AC")
                te_sq2[(oc, T)] = P.mm(
                    bank(4 + T, 0, 1), bf(OFF["ones"], 1), bf(sqb, 512),
                    start=(oc == 0), stop=(oc == NC_D - 1),
                    waits=[("AC", acs)], inc="TE")
                if T == 1:
                    dv_s2[oc] = P.cnt["DV"]
            te_w_done[oc] = P.cnt["TE"]

        te_all = P.cnt["TE"]
        P.act(ff(F["ms"], 512, 0, 1), bank(2, 0, 1), AF.Copy,
              scale=f32(1.0 / D), waits=[("TE", te_all)])
        P.act(ff(F["ms"] + 512, 512, 0, 1), bank(3, 0, 1), AF.Copy,
              scale=f32(1.0 / D))
        P.act(ff(F["sqs"], 512, 0, 1), bank(4, 0, 1), AF.Copy,
              scale=f32(1.0 / D))
        ac2 = P.act(ff(F["sqs"] + 512, 512, 0, 1), bank(5, 0, 1), AF.Copy,
                    scale=f32(1.0 / D), inc="AC")
        P.vec("tensor_mul", (ff(M2, 1024, 0, 1), ff(F["ms"], 1024, 0, 1),
                             ff(F["ms"], 1024, 0, 1)), waits=[("AC", ac2)])
        dv2 = P.vec("tensor_sub", (ff(F["sqs"], 1024, 0, 1),
                                   ff(F["sqs"], 1024, 0, 1),
                                   ff(M2, 1024, 0, 1)), inc="DV")
        ac3 = P.act(ff(F["sqs"], 1024, 0, 1), ff(F["sqs"], 1024, 0, 1),
                    AF.Sqrt, bias=f32(1e-5), waits=[("DV", dv2)], inc="AC")
        P.vec("reciprocal", (ff(F["sqs"], 1024, 0, 1),
                             ff(F["sqs"], 1024, 0, 1)),
              waits=[("AC", ac3)])
        P.vec("tensor_mul", (ff(F["ms"], 1024, 0, 1), ff(F["ms"], 1024, 0, 1),
                             ff(F["sqs"], 1024, 0, 1)))
        dv5 = P.vec("tensor_scalar_mul", (ff(F["ms"], 1024, 0, 1),
                                          ff(F["ms"], 1024, 0, 1), -1.0),
                    inc="DV")
        P.mm(bank(0), ff(F["onesk"], 128, 0, 1), ff(F["sqs"], 512, 0, 1),
             start=True, stop=True, waits=[("DV", dv5)])
        P.mm(bank(1), ff(F["onesk"], 128, 0, 1),
             ff(F["sqs"] + 512, 512, 0, 1), start=True, stop=True)
        P.mm(bank(6), ff(F["onesk"], 128, 0, 1), ff(F["ms"], 512, 0, 1),
             start=True, stop=True)
        teb2 = P.mm(bank(7), ff(F["onesk"], 128, 0, 1),
                    ff(F["ms"] + 512, 512, 0, 1), start=True, stop=True,
                    inc="TE")
        if debug:
            wstg = [("TE", teb2), ("LS", P.cnt["LS"])]
            P.act(bf(OFF["stg"], 512), bank(0), AF.Copy, waits=wstg)
            P.act(bf(OFF["stg"] + 512, 512), bank(1), AF.Copy)
            P.act(bf(OFF["stg"] + 1024, 512), bank(6), AF.Copy)
            acb3 = P.act(bf(OFF["stg"] + 1536, 512), bank(7), AF.Copy,
                         inc="AC")
            P.dma("s", dbg_bk[:, :], bf(OFF["stg"], 2048),
                  waits=[("AC", acb3)])
            teb2 = max(teb2, 0)
            dbg_gate = acb3
        else:
            dbg_gate = None
        ost_ls = [0, 0]
        for oc in range(NC_D):
            ob = F["ostg0"] if oc % 2 == 0 else F["ostg1"]
            dvv = 0
            for T in range(2):
                s2c = OFF["S2"] + oc * 1024 + T * 512
                wv_ = []
                if oc == 0 and T == 0:
                    wv_.append(("TE", teb2))
                    if dbg_gate is not None:
                        wv_.append(("AC", dbg_gate))
                if T == 0 and oc >= 2:
                    wv_.append(("ST", ost_ls[oc % 2]))
                P.vec("tensor_mul", (ff(ob + T * 512, 512), bf(s2c, 512),
                                     bank(T)), waits=wv_)
                dvv = P.vec("tensor_add", (ff(ob + T * 512, 512),
                                           ff(ob + T * 512, 512),
                                           bank(6 + T)), inc="DV")
            ost_ls[oc % 2] = P.dma("s", y[oc, :, :], ff(ob, 1024),
                                   waits=[("DV", dvv)], sem="ST")
        if debug:
            fin = P.snap()
            wf = [("TE", fin["TE"]), ("DV", fin["DV"]), ("AC", fin["AC"]),
                  ("ST", fin["ST"])]
            P.dma("s", dbg_x1, bf(OFF["S1"], 32768), waits=wf)
            P.dma("s", dbg_s2, bf(OFF["S2"], 32768))
        P.wait("s", [("ST", P.cnt["ST"]), ("LS", P.cnt["LS"])])

        P.cut()
        cut = P.cuts[phases - 100] if phases >= 100 else P.cuts[min(phases, 5) - 1] if phases <= 2 else P.cuts[-(6 - phases)]

        @block.tensor
        def _(t):
            for fn in P.ops["t"][:cut["t"]]:
                fn(t, sems)

        @block.scalar
        def _(a):
            for fn in P.ops["a"][:cut["a"]]:
                fn(a, sems)

        @block.vector
        def _(v):
            for fn in P.ops["v"][:cut["v"]]:
                fn(v, sems)

        @block.gpsimd
        def _(g):
            for fn in P.ops["g"][:cut["g"]]:
                fn(g, sems)

        @block.sync
        def _(s):
            for fn in P.ops["s"][:cut["s"]]:
                fn(s, sems)

    return nc


def _local_idx(j):
    return np.concatenate(
        [np.arange(128 * (4 * t + j), 128 * (4 * t + j) + 128)
         for t in range(8)])


def _host_inputs(L0_attn_hidden_states, L0_attn_Wq, L0_attn_Wk, L0_attn_Wv,
                 L0_attn_Wo, L0_ffn_lora_A, L0_ffn_lora_B, L0_ffn_W,
                 L0_ffn_bias, L0_ffn_lora_weights):
    x = np.asarray(L0_attn_hidden_states)
    C = np.ascontiguousarray
    shared = {
        "wkT": C(np.asarray(L0_attn_Wk).T),
        "wvT": C(np.asarray(L0_attn_Wv).T),
        "wqT": C(np.asarray(L0_attn_Wq).T),
        "woT": C(np.asarray(L0_attn_Wo).T),
        "wT": C(np.asarray(L0_ffn_W).T),
        "aallT": C(np.asarray(L0_ffn_lora_A).reshape(128, D).T),
        "ballT": C(np.asarray(L0_ffn_lora_B).transpose(0, 2, 1)
                   .reshape(128, D)),
        "bias2d": C(np.asarray(L0_ffn_bias).reshape(32, 128).T),
        "w2rep": np.repeat(np.asarray(L0_ffn_lora_weights), 16)
        .reshape(128, 1).astype(np.float32),
    }
    inv = 10000.0 ** (-(np.arange(64, dtype=np.float64)) / 64.0)
    ang = np.arange(S, dtype=np.float64)[None, :] * inv[:, None]
    cosh = np.cos(ang)
    sinh = np.sin(ang)
    cos_feat = np.concatenate([cosh, cosh], 0).astype(np.float32)
    ssgn_feat = np.concatenate([-sinh, sinh], 0).astype(np.float32)
    shared["cosk"] = C(cos_feat)
    shared["ssgnk"] = C(ssgn_feat)

    xT = [C(x[b].T) for b in range(B)]
    in_maps = []
    for c in range(8):
        b, j = c // 4, c % 4
        idx = _local_idx(j)
        m = dict(shared)
        m["xT"] = xT[b]
        m["xlT"] = C(xT[b][:, idx])
        m["cosq"] = C(cos_feat[:, idx])
        m["ssgnq"] = C(ssgn_feat[:, idx])
        pk = np.arange(128)[:, None]
        r = np.arange(512)[None, :]
        qpos = 128 * (4 * (r // 128) + j) + (r % 128)
        mask = np.zeros((128, 16 * 512), dtype=np.float32)
        for kc in range(16):
            mask[:, kc * 512:(kc + 1) * 512] = (128 * kc + pk <= qpos)
        m["maskin"] = mask
        in_maps.append(m)
    return in_maps


_cache = {}


def kernel(**inputs):
    in_maps = _host_inputs(**inputs)
    if "nc" not in _cache:
        _cache["nc"] = build()
    res = run_bass_kernel_spmd(_cache["nc"], in_maps, core_ids=list(range(8)))
    out = np.empty((B, S, D), dtype=np.float32)
    for c in range(8):
        b, j = c // 4, c % 4
        yv = np.asarray(res.results[c]["y"]).reshape(D, LT)
        out[b, _local_idx(j), :] = yv.T
    return out


# version 14
# speedup vs baseline: 1.0137x; 1.0137x over previous
import math
import numpy as np
import concourse.bass as bass
import concourse.mybir as mybir
from concourse.bass_utils import run_bass_kernel_spmd

F32 = mybir.dt.float32
BF16 = mybir.dt.bfloat16
AF = mybir.ActivationFunctionType
ALU = mybir.AluOpType
f32 = np.float32

B, S, D = 2, 4096, 4096
NH, NKV, HD = 32, 8, 128
NB = S // 128
LT = 1024
NC_D = D // 128
EXP_BIAS = -25.0
EXP_SCALE = 1.0 / math.sqrt(HD)

ENGINES = ("t", "a", "v", "g", "s")


class Plan:

    def __init__(self):
        self.ops = {e: [] for e in ENGINES}
        self.cnt = {e: 0 for e in ("TE", "AC", "DV", "LG", "LS", "ST", "XL")}

    def emit(self, eng, fn):
        self.ops[eng].append(fn)

    def bump(self, sem, amt):
        self.cnt[sem] += amt
        return self.cnt[sem]

    def _pre(self, e, sems, waits):
        for sname, val in waits:
            if val > 0:
                e.wait_ge(sems[sname], val)

    def mm(self, out, lhsT, rhs, start, stop, waits=(), inc=None):
        w = list(waits)
        val = self.bump(inc, 1) if inc else None

        def fn(e, sems, out=out, lhsT=lhsT, rhs=rhs, start=start, stop=stop,
               w=w, inc=inc):
            self._pre(e, sems, w)
            i = e.matmul(out, lhsT, rhs, start=start, stop=stop)
            if inc:
                i.then_inc(sems[inc], 1)
        self.emit("t", fn)
        return val

    def act(self, out, in_, func, bias=0.0, scale=1.0, waits=(), inc=None):
        w = list(waits)
        val = self.bump(inc, 1) if inc else None

        def fn(e, sems, out=out, in_=in_, func=func, bias=bias, scale=scale,
               w=w, inc=inc):
            self._pre(e, sems, w)
            i = e.activation(out, in_, func, bias=bias, scale=scale)
            if inc:
                i.then_inc(sems[inc], 1)
        self.emit("a", fn)
        return val

    def vec(self, name, args, waits=(), inc=None):
        w = list(waits)
        val = self.bump(inc, 1) if inc else None

        def fn(e, sems, name=name, args=args, w=w, inc=inc):
            self._pre(e, sems, w)
            i = getattr(e, name)(*args)
            if inc:
                i.then_inc(sems[inc], 1)
        self.emit("v", fn)
        return val

    def dma(self, q, out, in_, waits=(), sem=None):
        w = list(waits)
        sem = sem or ("LG" if q == "g" else "LS")
        val = self.bump(sem, 16)

        def fn(e, sems, out=out, in_=in_, w=w, sem=sem):
            self._pre(e, sems, w)
            e.dma_start(out=out, in_=in_).then_inc(sems[sem], 16)
        self.emit(q, fn)
        return val

    def wait(self, eng, waits):
        w = list(waits)

        def fn(e, sems, w=w):
            self._pre(e, sems, w)
        self.emit(eng, fn)

    def snap(self):
        return dict(self.cnt)

    def cut(self):
        if not hasattr(self, "cuts"):
            self.cuts = []
        self.cuts.append({e: len(self.ops[e]) for e in ENGINES})



def _wtile(src2d, ncols):
    return src2d.rearrange("(c p) o -> p c o", p=128)


def _wdst(dst2d, c, o):
    return dst2d.rearrange("p (c o) -> p c o", c=c)

def build(debug=False, phases=5, dbg_early=False):
    nc = bass.Bass("TRN2", target_bir_lowering=False)
    P = Plan()

    ap = lambda t: t.ap()
    xT = ap(nc.dram_tensor("xT", [D, S], F32, kind="ExternalInput"))
    xlT = ap(nc.dram_tensor("xlT", [D, LT], F32, kind="ExternalInput"))
    wkT = ap(nc.dram_tensor("wkT", [D, 1024], F32, kind="ExternalInput"))
    wvT = ap(nc.dram_tensor("wvT", [D, 1024], F32, kind="ExternalInput"))
    wqT = ap(nc.dram_tensor("wqT", [D, D], F32, kind="ExternalInput"))
    woT = ap(nc.dram_tensor("woT", [D, D], F32, kind="ExternalInput"))
    wT = ap(nc.dram_tensor("wT", [D, D], F32, kind="ExternalInput"))
    aallT = ap(nc.dram_tensor("aallT", [D, 128], F32, kind="ExternalInput"))
    ballT = ap(nc.dram_tensor("ballT", [128, D], F32, kind="ExternalInput"))
    bias2d = ap(nc.dram_tensor("bias2d", [128, 32], F32, kind="ExternalInput"))
    w2rep = ap(nc.dram_tensor("w2rep", [128, 1], F32, kind="ExternalInput"))
    cosk = ap(nc.dram_tensor("cosk", [128, S], F32, kind="ExternalInput"))
    ssgnk = ap(nc.dram_tensor("ssgnk", [128, S], F32, kind="ExternalInput"))
    cosq = ap(nc.dram_tensor("cosq", [128, LT], F32, kind="ExternalInput"))
    ssgnq = ap(nc.dram_tensor("ssgnq", [128, LT], F32, kind="ExternalInput"))
    maskin = ap(nc.dram_tensor("maskin", [128, 16 * 512], F32,
                               kind="ExternalInput"))
    y = ap(nc.dram_tensor("y", [NC_D, 128, LT], F32, kind="ExternalOutput"))

    scr_kind = "ExternalOutput" if debug else "Internal"
    k_scr = ap(nc.dram_tensor("k_scr", [8, 128, S], BF16, kind=scr_kind))
    v_scr = ap(nc.dram_tensor("v_scr", [NB, 128, 1024], BF16, kind=scr_kind))
    q_scr = ap(nc.dram_tensor("q_scr", [NH, 128, LT], BF16, kind=scr_kind))
    at_scr = ap(nc.dram_tensor("at_scr", [NH, 128, LT], BF16, kind=scr_kind))
    dbg_x1 = dbg_s2 = dbg_bk = None
    if debug:
        dbg_x1 = ap(nc.dram_tensor("dbg_x1", [128, 32768], BF16,
                                   kind="ExternalOutput"))
        dbg_s2 = ap(nc.dram_tensor("dbg_s2", [128, 32768], BF16,
                                   kind="ExternalOutput"))
        dbg_bk = ap(nc.dram_tensor("dbg_bk", [128, 2048], BF16,
                                   kind="ExternalOutput"))

    ABF_N = 90112
    AF32_N = 6656

    OFF = dict(
        wk=0, wv=32768, xb0=65536, xb1=65536 + 8192,

        xl=0, wqb0=32768, wqb1=32768 + 4096, cosq=40960, ssgnq=40960 + 1024,
        V=0, pt=32768, K=40960, mask=73728, qbA0=81920,
        qbB0=82944, qbA1=83968, qbB1=84992, pt_unused=0,
        attn=0, S1=32768, wob0=65536, wob1=65536 + 4096, sq0=73728,
        sq1=73728 + 512,
        S2=0, ab=73728, bb=77824, sq5_0=81920, sq5_1=81920 + 512, hsc=82944,
        recb=84992,
        stg=86016, ones=88064, onesm=88192, praw0=38912, praw1=39936,
    )
    F = dict(xlres0=0, xlres1=1024, ostg0=2048, ostg1=3072, rtmp0=4096,
             rtmp1=4608, bias=5120, w2=5152, onesk=5153, dsb=5281, rec=5793,
             ms=5281, sqs=4096,
             coskt0=2048, coskt1=2304, ssgnkt0=2560, ssgnkt1=2816,
             cosq=0, ssgnq=1024)

    with (
        nc.sbuf_tensor("abf", [128, ABF_N], BF16) as abf_t,
        nc.sbuf_tensor("af32", [128, AF32_N], F32) as af32_t,
        nc.psum_tensor("psum", [128, 4096], F32) as psum_t,
        nc.semaphore("TE") as sTE,
        nc.semaphore("AC") as sAC,
        nc.semaphore("DV") as sDV,
        nc.semaphore("LG") as sLG,
        nc.semaphore("LS") as sLS,
        nc.semaphore("ST") as sST,
        nc.semaphore("XL") as sXL,
        nc.Block() as block,
    ):
        A = abf_t[:]
        AF_ = af32_t[:]
        PS = psum_t[:]
        sems = {"TE": sTE, "AC": sAC, "DV": sDV, "LG": sLG, "LS": sLS,
                "ST": sST, "XL": sXL}

        def bank(i, p0=0, p1=128, n=512):
            return PS[p0:p1, i * 512:i * 512 + n]

        def bf(off, n, p0=0, p1=128):
            return A[p0:p1, off:off + n]

        def ff(off, n, p0=0, p1=128):
            return AF_[p0:p1, off:off + n]

        stg_ls = [0, 0, 0, 0]
        stg_i = [0]

        def stage(dv_val, scr_ap, width):
            s = stg_i[0] % 4
            stg_i[0] += 1
            v = P.dma("s", scr_ap, bf(OFF["stg"] + s * 512, width),
                      waits=[("DV", dv_val)])
            stg_ls[s] = v

        def next_stage():
            s = stg_i[0] % 4
            return s, stg_ls[s]

        NT1_ = 16
        xload = [0] * NT1_
        cload = [0] * NT1_

        def p1_load(i, w=None):
            xb = OFF["xb0"] if i % 2 == 0 else OFF["xb1"]
            P.dma("g", _wdst(bf(xb, 8192), 32, 256),
                  xT[:, i * 256:(i + 1) * 256].rearrange(
                      "(c p) t -> p c t", p=128), waits=(w or []))
            ck = F["coskt0"] if i % 2 == 0 else F["coskt1"]
            sk = F["ssgnkt0"] if i % 2 == 0 else F["ssgnkt1"]
            P.dma("g", ff(ck, 256), cosk[:, i * 256:(i + 1) * 256])
            cload[i] = P.dma("g", ff(sk, 256), ssgnk[:, i * 256:(i + 1) * 256])
            xload[i] = cload[i]

        p1_load(0)
        p1_load(1)
        c_wk = P.dma("g", _wdst(bf(OFF["wk"], 32768), 32, 1024),
                     wkT.rearrange("(c p) o -> p c o", p=128))
        c_pre = P.dma("g", _wdst(bf(OFF["wv"], 32768), 32, 1024),
                      wvT.rearrange("(c p) o -> p c o", p=128))
        wk_part = [c_wk] * 4
        wv_part = [c_pre] * 4
        P.dma("g", ff(F["bias"], 32), bias2d[:, :])
        c_w2 = P.dma("g", ff(F["w2"], 1), w2rep[:, :])
        P.vec("tensor_scalar_mul", (ff(F["w2"], 1), ff(F["w2"], 1), 2.0),
              waits=[("LG", c_w2)], inc="DV")
        P.vec("memset", (bf(OFF["ones"], 1), 1.0), inc="DV")
        P.vec("memset", (bf(OFF["onesm"], 128), 1.0), inc="DV")
        P.vec("memset", (ff(F["onesk"], 128, 0, 1), 1.0), inc="DV")

        NT1 = 16
        kv_te_done = [0] * NT1
        dv_done = {}

        kvmm = 0
        for i in range(NT1):
            xb = OFF["xb0"] if i % 2 == 0 else OFF["xb1"]
            ck = F["coskt0"] if i % 2 == 0 else F["coskt1"]
            sk = F["ssgnkt0"] if i % 2 == 0 else F["ssgnkt1"]
            for oh in range(8):
                bk = kvmm % 2
                kvmm += 1
                w = [("LG", max(xload[i], wk_part[oh // 2]))]
                key = ("kb", bk)
                if key in dv_done:
                    w.append(("DV", dv_done[key]))
                for dc in range(32):
                    te = P.mm(bank(bk, n=256),
                              bf(OFF["wk"] + dc * 1024 + oh * 128, 128),
                              bf(xb + dc * 256, 256),
                              start=(dc == 0), stop=(dc == 31),
                              waits=(w if dc == 0 else ()),
                              inc=("TE" if dc == 31 else None))
                ssl, need_ls = next_stage()
                wv_ = [("TE", te), ("LS", need_ls)]
                P.vec("tensor_mul", (ff(F["rtmp0"], 256, 0, 64),
                                     bank(bk, 64, 128, 256),
                                     ff(sk, 256, 0, 64)), waits=wv_)
                P.vec("tensor_mul", (ff(F["rtmp0"], 256, 64, 128),
                                     bank(bk, 0, 64, 256),
                                     ff(sk, 256, 64, 128)))
                P.vec("tensor_mul", (ff(F["rtmp1"], 256), bank(bk, n=256),
                                     ff(ck, 256)))
                dvv = P.vec("tensor_add", (bf(OFF["stg"] + ssl * 512, 256),
                                           ff(F["rtmp1"], 256),
                                           ff(F["rtmp0"], 256)), inc="DV")
                dv_done[("kb", bk)] = dvv
                stage(dvv, k_scr[oh, :, i * 256:i * 256 + 256], 256)
            for tb in range(2):
                for ov in range(2):
                    bv = 2 + kvmm % 2
                    kvmm += 1
                    w = [("LG", max(xload[i], wv_part[2 * ov + 1]))]
                    key = ("vb", bv)
                    if key in dv_done:
                        w.append(("DV", dv_done[key]))
                    for dc in range(32):
                        te = P.mm(bank(bv),
                                  bf(xb + dc * 256 + tb * 128, 128),
                                  bf(OFF["wv"] + dc * 1024 + ov * 512, 512),
                                  start=(dc == 0), stop=(dc == 31),
                                  waits=(w if dc == 0 else ()),
                                  inc=("TE" if dc == 31 else None))
                    ssl, need_ls = next_stage()
                    dvv = P.vec("tensor_copy",
                                (bf(OFF["stg"] + ssl * 512, 512), bank(bv)),
                                waits=[("TE", te), ("LS", need_ls)], inc="DV")
                    dv_done[("vb", bv)] = dvv
                    gb = i * 2 + tb
                    stage(dvv, v_scr[gb, :, ov * 512:ov * 512 + 512], 512)
            kv_te_done[i] = P.cnt["TE"]
            if i + 2 < NT1:
                p1_load(i + 2, w=[("TE", kv_te_done[i])])

        bar1 = P.snap()
        P.cut()
        xl_a = P.dma("g", _wdst(bf(OFF["xl"], 16384), 16, 1024),
                     xlT[0:2048, :].rearrange("(c p) t -> p c t", p=128),
                     waits=[("TE", bar1["TE"]), ("DV", bar1["DV"])],
                     sem="XL")
        xl_b = P.dma("g", _wdst(bf(OFF["xl"] + 16384, 16384), 16, 1024),
                     xlT[2048:4096, :].rearrange("(c p) t -> p c t", p=128),
                     sem="XL")
        xl_part = [xl_a] * 4 + [xl_b] * 4
        P.dma("g", ff(F["cosq"], 1024), cosq[:, :])
        c_q = P.dma("g", ff(F["ssgnq"], 1024), ssgnq[:, :])

        wq_ls = [0, 0]
        q_dv = {}
        q_te_done = {}
        q_st_ls = {}
        for oh in range(NH):
            buf = OFF["wqb0"] if oh % 2 == 0 else OFF["wqb1"]
            w = [("LG", c_q)] if oh < 2 else [("TE", q_te_done[oh - 2])]
            wq_ls[oh % 2] = P.dma(
                "g", _wdst(bf(buf, 4096), 32, 128),
                wqT[:, oh * 128:(oh + 1) * 128].rearrange(
                    "(c p) o -> p c o", p=128), waits=w)
            for T in range(2):
                bk = T
                wmm = [("LG", wq_ls[oh % 2])]
                key = ("qb", bk)
                if key in q_dv:
                    wmm.append(("DV", q_dv[key]))
                for dc in range(32):
                    wdc = list(wmm) if dc == 0 else []
                    if oh == 0 and T == 0 and dc % 4 == 0:
                        wdc.append(("XL", xl_part[dc // 4]))
                    te = P.mm(bank(bk), bf(buf + dc * 128, 128),
                              bf(OFF["xl"] + dc * 1024 + T * 512, 512),
                              start=(dc == 0), stop=(dc == 31),
                              waits=wdc,
                              inc=("TE" if dc == 31 else None))
                ssl, need_ls = next_stage()
                wv_ = [("TE", te), ("LS", need_ls)]
                P.vec("tensor_mul", (ff(F["rtmp0"], 512, 0, 64),
                                     bank(bk, 64, 128),
                                     ff(F["ssgnq"] + T * 512, 512, 0, 64)),
                      waits=wv_)
                P.vec("tensor_mul", (ff(F["rtmp0"], 512, 64, 128),
                                     bank(bk, 0, 64),
                                     ff(F["ssgnq"] + T * 512, 512, 64, 128)))
                P.vec("tensor_mul", (ff(F["rtmp1"], 512), bank(bk),
                                     ff(F["cosq"] + T * 512, 512)))
                dvv = P.vec("tensor_add", (bf(OFF["stg"] + ssl * 512, 512),
                                           ff(F["rtmp1"], 512),
                                           ff(F["rtmp0"], 512)), inc="DV")
                q_dv[("qb", bk)] = dvv
                stage(dvv, q_scr[oh, :, T * 512:T * 512 + 512], 512)
            q_st_ls[oh] = P.cnt["LS"]
            q_te_done[oh] = P.cnt["TE"]

        if debug and dbg_early:
            fe = P.snap()
            P.dma("s", dbg_s2, bf(OFF["xl"], 32768),
                  waits=[("TE", fe["TE"]), ("DV", fe["DV"]), ("LG", fe["LG"])])
            P.wait("s", [("LS", P.cnt["LS"])])
        bar2 = P.snap()
        P.cut()
        wbar = [("TE", bar2["TE"]), ("DV", bar2["DV"])]
        wbar.append(("LS", bar2["LS"]))
        P.dma("s", _wdst(bf(OFF["K"], 32768), 8, 4096),
              k_scr.rearrange("h p t -> p h t"), waits=wbar)
        c_kv = P.dma("s", _wdst(bf(OFF["V"], 32768), 32, 1024),
                     v_scr.rearrange("b p o -> p b o"))
        v_part = [c_kv] * 4
        c_mask = P.dma("g", bf(OFF["mask"], 8192), maskin[:, :],
                       waits=[("TE", bar2["TE"]), ("DV", bar2["DV"])])

        LAG = 3
        NPT = 6
        q_ls = [0, 0]
        m = 0
        ac_exp = {}
        te_sb = {}
        te_pvp = {}
        p_ready = {}
        pair_last_te = {}
        tail_evac = {}
        tail_recip = {}
        tail_acp = {}
        stg3_ls = [0, 0]
        gT = 0
        def q_prefetch(gg):
            qa_ = OFF["qbA0"] if gg % 2 == 0 else OFF["qbA1"]
            qbb = OFF["qbB0"] if gg % 2 == 0 else OFF["qbB1"]
            wq_ = [("LS", max(q_st_ls[2 * gg + 1], c_kv))]
            if gg >= 2:
                wq_.append(("TE", pair_last_te[gg - 2]))
            P.dma("s", bf(qa_, 1024), q_scr[2 * gg, :, :], waits=wq_)
            q_ls[gg % 2] = P.dma("s", bf(qbb, 1024), q_scr[2 * gg + 1, :, :])

        q_prefetch(0)
        for g in range(16):
            hA, hB = 2 * g, 2 * g + 1
            kvh = g // 2
            qa = OFF["qbA0"] if g % 2 == 0 else OFF["qbA1"]
            qb_ = OFF["qbB0"] if g % 2 == 0 else OFF["qbB1"]
            if g + 1 < 16:
                q_prefetch(g + 1)
            for T in range(2):
                nch = 16 * (T + 1)
                mbase = m
                for c in range(nch + LAG):
                    if c < nch:
                        mm_ = m
                        m += 1
                        sb2 = 2 * (mm_ % 2)
                        wsc = [("LS", q_ls[g % 2])]
                        if mm_ >= 2:
                            wsc.append(("AC", ac_exp[mm_ - 2]))
                        P.mm(bank(sb2),
                             bf(OFF["K"] + kvh * 4096 + c * 128, 128),
                             bf(qa + T * 512, 512), start=True, stop=True,
                             waits=wsc)
                        te_sb[mm_] = P.mm(
                            bank(sb2 + 1),
                            bf(OFF["K"] + kvh * 4096 + c * 128, 128),
                            bf(qb_ + T * 512, 512), start=True, stop=True,
                            inc="TE")
                        pt = OFF["pt"] + (mm_ % NPT) * 1024
                        wex = [("TE", te_sb[mm_])]
                        if mm_ >= NPT:
                            wex.append(("TE", te_pvp[mm_ - NPT]))
                        ac_exp[mm_] = P.act(
                            bf(pt, 1024), PS[:, sb2 * 512:sb2 * 512 + 1024],
                            AF.Exp, bias=f32(EXP_BIAS), scale=f32(EXP_SCALE),
                            waits=wex, inc="AC")
                        if c >= 16 * T:
                            rel = c - 16 * T
                            P.vec("tensor_mul",
                                  (bf(pt, 512), bf(pt, 512),
                                   bf(OFF["mask"] + rel * 512, 512)),
                                  waits=[("AC", ac_exp[mm_]), ("LG", c_mask)])
                            dvm = P.vec(
                                "tensor_mul",
                                (bf(pt + 512, 512), bf(pt + 512, 512),
                                 bf(OFF["mask"] + rel * 512, 512)), inc="DV")
                            p_ready[mm_] = ("DV", dvm)
                        else:
                            p_ready[mm_] = ("AC", ac_exp[mm_])
                    if c >= LAG:
                        cc = c - LAG
                        mcc = mbase + cc
                        ptc = OFF["pt"] + (mcc % NPT) * 1024
                        wpv = [p_ready[mcc], ("LS", v_part[cc // 8])]
                        if cc == 0 and gT >= 1:
                            wpv.append(("AC", tail_acp[gT - 1]))
                        P.mm(bank(4),
                             bf(OFF["V"] + cc * 1024 + kvh * 128, 128),
                             bf(ptc, 512), start=(cc == 0),
                             stop=(cc == nch - 1), waits=wpv)
                        wdn = []
                        if cc == 0 and gT >= 1:
                            wdn.append(("DV", tail_recip[gT - 1]))
                        P.mm(bank(6), bf(OFF["onesm"], 128),
                             bf(ptc, 512), start=(cc == 0),
                             stop=(cc == nch - 1), waits=wdn)
                        P.mm(bank(5),
                             bf(OFF["V"] + cc * 1024 + kvh * 128, 128),
                             bf(ptc + 512, 512), start=(cc == 0),
                             stop=(cc == nch - 1))
                        te_pvp[mcc] = P.mm(
                            bank(7), bf(OFF["onesm"], 128),
                            bf(ptc + 512, 512), start=(cc == 0),
                            stop=(cc == nch - 1), inc="TE")
                ssl = gT % 2
                rb = 0 if gT % 2 == 0 else 1024
                praw = OFF["praw0"] if gT % 2 == 0 else OFF["praw1"]
                wacp = [("TE", te_pvp[mbase + nch - 1])]
                if gT >= 2:
                    wacp.append(("DV", tail_evac[gT - 2]))
                tail_acp[gT] = P.act(bf(praw, 1024), PS[:, 2048:3072],
                                     AF.Copy, waits=wacp, inc="AC")
                P.vec("reciprocal", (ff(rb, 512), bank(6)),
                      waits=[("TE", te_pvp[mbase + nch - 1])])
                tail_recip[gT] = P.vec("reciprocal",
                                       (ff(rb + 512, 512), bank(7)), inc="DV")
                wmu = [("AC", tail_acp[gT]), ("LS", stg3_ls[ssl])]
                P.vec("tensor_mul", (bf(OFF["stg"] + ssl * 1024, 512),
                                     bf(praw, 512), ff(rb, 512)), waits=wmu)
                tail_evac[gT] = P.vec(
                    "tensor_mul", (bf(OFF["stg"] + ssl * 1024 + 512, 512),
                                   bf(praw + 512, 512), ff(rb + 512, 512)),
                    inc="DV")
                P.dma("s", at_scr[hA, :, T * 512:T * 512 + 512],
                      bf(OFF["stg"] + ssl * 1024, 512),
                      waits=[("DV", tail_evac[gT])])
                stg3_ls[ssl] = P.dma(
                    "s", at_scr[hB, :, T * 512:T * 512 + 512],
                    bf(OFF["stg"] + ssl * 1024 + 512, 512))
                gT += 1
                P.cut()
            pair_last_te[g] = P.cnt["TE"]

        bar3 = P.snap()
        P.cut()
        c_attn = P.dma("s", _wdst(bf(OFF["attn"], 32768), 32, 1024),
                       at_scr.rearrange("h p t -> p h t"),
                       waits=[("TE", bar3["TE"]), ("DV", bar3["DV"]),
                              ("AC", bar3["AC"]), ("LS", bar3["LS"])])
        attn_part = [c_attn] * 8

        wo_banks = {0: (0, 1), 1: (6, 7)}
        wo_ls = [0, 0]
        xl_ls = [0, 0]
        dv_s1 = {}
        te_sq = {}
        te_wo_done = {}
        for oc in range(NC_D):
            bufw = OFF["wob0"] if oc % 2 == 0 else OFF["wob1"]
            w = ([("DV", bar3["DV"]), ("TE", bar3["TE"])] if oc < 2
                 else [("TE", te_wo_done[oc - 2])])
            wo_ls[oc % 2] = P.dma(
                "g", _wdst(bf(bufw, 4096), 32, 128),
                woT[:, oc * 128:(oc + 1) * 128].rearrange(
                    "(c p) o -> p c o", p=128), waits=w)
            wx = [("DV", dv_s1[oc - 2])] if oc >= 2 else []
            xl_ls[oc % 2] = P.dma(
                "s", ff(F["xlres0"] if oc % 2 == 0 else F["xlres1"], 1024),
                xlT[oc * 128:(oc + 1) * 128, :], waits=wx)
            for T in range(2):
                bk = wo_banks[oc % 2][T]
                wmm = [("LG", wo_ls[oc % 2])]
                if oc >= 2:
                    wmm.append(("DV", dv_s1[oc - 2]))
                for dc in range(32):
                    wdc = list(wmm) if dc == 0 else []
                    if oc == 0 and T == 0 and dc % 4 == 0:
                        wdc.append(("LS", attn_part[dc // 4]))
                    te = P.mm(bank(bk), bf(bufw + dc * 128, 128),
                              bf(OFF["attn"] + dc * 1024 + T * 512, 512),
                              start=(dc == 0), stop=(dc == 31),
                              waits=wdc,
                              inc=("TE" if dc == 31 else None))
                s1c = OFF["S1"] + oc * 1024 + T * 512
                dvv = P.vec("tensor_add",
                            (bf(s1c, 512), bank(bk),
                             ff((F["xlres0"] if oc % 2 == 0 else F["xlres1"])
                                + T * 512, 512)),
                            waits=[("TE", te), ("LS", xl_ls[oc % 2])],
                            inc="DV")
                P.mm(bank(2 + T, 0, 1), bf(OFF["ones"], 1), bf(s1c, 512),
                     start=(oc == 0), stop=(oc == NC_D - 1),
                     waits=[("DV", dvv)])
                sqb = OFF["sq0"] if T == 0 else OFF["sq1"]
                wsq = [("DV", dvv)]
                if oc >= 1:
                    wsq.append(("TE", te_sq[(oc - 1, T)]))
                acs = P.act(bf(sqb, 512), bf(s1c, 512), AF.Square,
                            waits=wsq, inc="AC")
                te_sq[(oc, T)] = P.mm(
                    bank(4 + T, 0, 1), bf(OFF["ones"], 1), bf(sqb, 512),
                    start=(oc == 0), stop=(oc == NC_D - 1),
                    waits=[("AC", acs)], inc="TE")
                if T == 1:
                    dv_s1[oc] = P.cnt["DV"]
            te_wo_done[oc] = P.cnt["TE"]

        def ln_finalize(norm_src_off, out_f32):
            te_all = P.cnt["TE"]
            ac1 = P.act(ff(F["ms"], 512, 0, 1), bank(2, 0, 1), AF.Copy,
                        scale=f32(1.0 / D), waits=[("TE", te_all)], inc="AC")
            P.act(ff(F["ms"] + 512, 512, 0, 1), bank(3, 0, 1), AF.Copy,
                  scale=f32(1.0 / D))
            P.act(ff(F["sqs"], 512, 0, 1), bank(4, 0, 1), AF.Copy,
                  scale=f32(1.0 / D))
            ac2 = P.act(ff(F["sqs"] + 512, 512, 0, 1), bank(5, 0, 1), AF.Copy,
                        scale=f32(1.0 / D), inc="AC")
            dv1 = P.vec("tensor_mul", (ff(F["dsb"] + 0, 1024, 0, 1),
                                       ff(F["ms"], 1024, 0, 1),
                                       ff(F["ms"], 1024, 0, 1)),
                        waits=[("AC", ac2)], inc="DV")
            return ac1, dv1

        M2 = 2048
        te_all = P.cnt["TE"]
        P.act(ff(F["ms"], 512, 0, 1), bank(2, 0, 1), AF.Copy,
              scale=f32(1.0 / D), waits=[("TE", te_all)])
        P.act(ff(F["ms"] + 512, 512, 0, 1), bank(3, 0, 1), AF.Copy,
              scale=f32(1.0 / D))
        P.act(ff(F["sqs"], 512, 0, 1), bank(4, 0, 1), AF.Copy,
              scale=f32(1.0 / D))
        ac2 = P.act(ff(F["sqs"] + 512, 512, 0, 1), bank(5, 0, 1), AF.Copy,
                    scale=f32(1.0 / D), inc="AC")
        P.vec("tensor_mul", (ff(M2, 1024, 0, 1), ff(F["ms"], 1024, 0, 1),
                             ff(F["ms"], 1024, 0, 1)), waits=[("AC", ac2)])
        dv2 = P.vec("tensor_sub", (ff(F["sqs"], 1024, 0, 1),
                                   ff(F["sqs"], 1024, 0, 1),
                                   ff(M2, 1024, 0, 1)), inc="DV")
        ac3 = P.act(ff(F["sqs"], 1024, 0, 1), ff(F["sqs"], 1024, 0, 1),
                    AF.Sqrt, bias=f32(1e-5), waits=[("DV", dv2)], inc="AC")
        P.vec("reciprocal", (ff(F["sqs"], 1024, 0, 1),
                             ff(F["sqs"], 1024, 0, 1)),
              waits=[("AC", ac3)])
        P.vec("tensor_mul", (ff(F["ms"], 1024, 0, 1), ff(F["ms"], 1024, 0, 1),
                             ff(F["sqs"], 1024, 0, 1)))
        dv5 = P.vec("tensor_scalar_mul", (ff(F["ms"], 1024, 0, 1),
                                          ff(F["ms"], 1024, 0, 1), -1.0),
                    inc="DV")
        P.mm(bank(0), ff(F["onesk"], 128, 0, 1), ff(F["sqs"], 512, 0, 1),
             start=True, stop=True, waits=[("DV", dv5)])
        P.mm(bank(1), ff(F["onesk"], 128, 0, 1),
             ff(F["sqs"] + 512, 512, 0, 1), start=True, stop=True)
        P.mm(bank(6), ff(F["onesk"], 128, 0, 1), ff(F["ms"], 512, 0, 1),
             start=True, stop=True)
        teb1 = P.mm(bank(7), ff(F["onesk"], 128, 0, 1),
                    ff(F["ms"] + 512, 512, 0, 1), start=True, stop=True,
                    inc="TE")
        dv_norm_oc = {}
        for T in range(2):
            for oc in range(NC_D):
                s1c = OFF["S1"] + oc * 1024 + T * 512
                P.vec("tensor_mul", (bf(s1c, 512), bf(s1c, 512), bank(T)),
                      waits=([("TE", teb1)] if (oc == 0 and T == 0) else ()))
                dv_norm_oc[(T, oc)] = P.vec(
                    "tensor_add", (bf(s1c, 512), bf(s1c, 512),
                                   bank(6 + T)), inc="DV")
        dv_norm = dv_norm_oc[(1, NC_D - 1)]

        bar4 = P.snap()
        P.cut()
        P.dma("g", _wdst(bf(OFF["ab"], 4096), 32, 128),
              aallT.rearrange("(c p) o -> p c o", p=128),
              waits=[("TE", bar4["TE"]), ("DV", bar4["DV"]),
                     ("AC", bar4["AC"])])
        c_lora = P.dma("g", bf(OFF["bb"], 4096), ballT[:, :])

        hsc_ac = 0
        for T in range(2):
            for dc in range(32):
                wdc = [("DV", dv_norm_oc[(T, dc)])]
                if dc == 0:
                    wdc.append(("LG", c_lora))
                te = P.mm(bank(2 + T), bf(OFF["ab"] + dc * 128, 128),
                          bf(OFF["S1"] + dc * 1024 + T * 512, 512),
                          start=(dc == 0), stop=(dc == 31),
                          waits=wdc,
                          inc=("TE" if dc == 31 else None))
            hsc_ac = P.act(bf(OFF["hsc"] + T * 512, 512), bank(2 + T),
                           AF.Copy, scale=ff(F["w2"], 1),
                           waits=[("TE", te)], inc="AC")

        w_ls = [0, 0]
        dv_s2 = {}
        te_sq2 = {}
        te_w_done = {}
        for oc in range(NC_D):
            bufw = OFF["wob0"] if oc % 2 == 0 else OFF["wob1"]
            w = ([("AC", hsc_ac)] if oc < 2 else
                 [("TE", te_w_done[oc - 2])])
            w_ls[oc % 2] = P.dma(
                "g", _wdst(bf(bufw, 4096), 32, 128),
                wT[:, oc * 128:(oc + 1) * 128].rearrange(
                    "(c p) o -> p c o", p=128), waits=w)
            for T in range(2):
                bk = wo_banks[oc % 2][T]
                wmm = [("LG", w_ls[oc % 2]), ("AC", hsc_ac)]
                if oc >= 2:
                    wmm.append(("DV", dv_s2[oc - 2]))
                if oc < 2:
                    wmm.append(("DV", dv_norm))
                for dc in range(32):
                    P.mm(bank(bk), bf(bufw + dc * 128, 128),
                         bf(OFF["S1"] + dc * 1024 + T * 512, 512),
                         start=(dc == 0), stop=False,
                         waits=(wmm if dc == 0 else ()))
                te = P.mm(bank(bk), bf(OFF["bb"] + oc * 128, 128),
                          bf(OFF["hsc"] + T * 512, 512),
                          start=False, stop=True, inc="TE")
                s2c = OFF["S2"] + oc * 1024 + T * 512
                dvv = P.vec("scalar_tensor_tensor",
                            (bf(s2c, 512), bank(bk), ff(F["bias"] + oc, 1),
                             bf(OFF["S1"] + oc * 1024 + T * 512, 512),
                             ALU.add, ALU.add),
                            waits=[("TE", te)], inc="DV")
                P.mm(bank(2 + T, 0, 1), bf(OFF["ones"], 1), bf(s2c, 512),
                     start=(oc == 0), stop=(oc == NC_D - 1),
                     waits=[("DV", dvv)])
                sqb = OFF["sq5_0"] if T == 0 else OFF["sq5_1"]
                wsq = [("DV", dvv)]
                if oc >= 1:
                    wsq.append(("TE", te_sq2[(oc - 1, T)]))
                acs = P.act(bf(sqb, 512), bf(s2c, 512), AF.Square,
                            waits=wsq, inc="AC")
                te_sq2[(oc, T)] = P.mm(
                    bank(4 + T, 0, 1), bf(OFF["ones"], 1), bf(sqb, 512),
                    start=(oc == 0), stop=(oc == NC_D - 1),
                    waits=[("AC", acs)], inc="TE")
                if T == 1:
                    dv_s2[oc] = P.cnt["DV"]
            te_w_done[oc] = P.cnt["TE"]

        te_all = P.cnt["TE"]
        P.act(ff(F["ms"], 512, 0, 1), bank(2, 0, 1), AF.Copy,
              scale=f32(1.0 / D), waits=[("TE", te_all)])
        P.act(ff(F["ms"] + 512, 512, 0, 1), bank(3, 0, 1), AF.Copy,
              scale=f32(1.0 / D))
        P.act(ff(F["sqs"], 512, 0, 1), bank(4, 0, 1), AF.Copy,
              scale=f32(1.0 / D))
        ac2 = P.act(ff(F["sqs"] + 512, 512, 0, 1), bank(5, 0, 1), AF.Copy,
                    scale=f32(1.0 / D), inc="AC")
        P.vec("tensor_mul", (ff(M2, 1024, 0, 1), ff(F["ms"], 1024, 0, 1),
                             ff(F["ms"], 1024, 0, 1)), waits=[("AC", ac2)])
        dv2 = P.vec("tensor_sub", (ff(F["sqs"], 1024, 0, 1),
                                   ff(F["sqs"], 1024, 0, 1),
                                   ff(M2, 1024, 0, 1)), inc="DV")
        ac3 = P.act(ff(F["sqs"], 1024, 0, 1), ff(F["sqs"], 1024, 0, 1),
                    AF.Sqrt, bias=f32(1e-5), waits=[("DV", dv2)], inc="AC")
        P.vec("reciprocal", (ff(F["sqs"], 1024, 0, 1),
                             ff(F["sqs"], 1024, 0, 1)),
              waits=[("AC", ac3)])
        P.vec("tensor_mul", (ff(F["ms"], 1024, 0, 1), ff(F["ms"], 1024, 0, 1),
                             ff(F["sqs"], 1024, 0, 1)))
        dv5 = P.vec("tensor_scalar_mul", (ff(F["ms"], 1024, 0, 1),
                                          ff(F["ms"], 1024, 0, 1), -1.0),
                    inc="DV")
        P.mm(bank(0), ff(F["onesk"], 128, 0, 1), ff(F["sqs"], 512, 0, 1),
             start=True, stop=True, waits=[("DV", dv5)])
        P.mm(bank(1), ff(F["onesk"], 128, 0, 1),
             ff(F["sqs"] + 512, 512, 0, 1), start=True, stop=True)
        P.mm(bank(6), ff(F["onesk"], 128, 0, 1), ff(F["ms"], 512, 0, 1),
             start=True, stop=True)
        teb2 = P.mm(bank(7), ff(F["onesk"], 128, 0, 1),
                    ff(F["ms"] + 512, 512, 0, 1), start=True, stop=True,
                    inc="TE")
        if debug:
            wstg = [("TE", teb2), ("LS", P.cnt["LS"])]
            P.act(bf(OFF["stg"], 512), bank(0), AF.Copy, waits=wstg)
            P.act(bf(OFF["stg"] + 512, 512), bank(1), AF.Copy)
            P.act(bf(OFF["stg"] + 1024, 512), bank(6), AF.Copy)
            acb3 = P.act(bf(OFF["stg"] + 1536, 512), bank(7), AF.Copy,
                         inc="AC")
            P.dma("s", dbg_bk[:, :], bf(OFF["stg"], 2048),
                  waits=[("AC", acb3)])
            teb2 = max(teb2, 0)
            dbg_gate = acb3
        else:
            dbg_gate = None
        ost_ls = [0, 0]
        for oc in range(NC_D):
            ob = F["ostg0"] if oc % 2 == 0 else F["ostg1"]
            dvv = 0
            for T in range(2):
                s2c = OFF["S2"] + oc * 1024 + T * 512
                wv_ = []
                if oc == 0 and T == 0:
                    wv_.append(("TE", teb2))
                    if dbg_gate is not None:
                        wv_.append(("AC", dbg_gate))
                if T == 0 and oc >= 2:
                    wv_.append(("ST", ost_ls[oc % 2]))
                P.vec("tensor_mul", (ff(ob + T * 512, 512), bf(s2c, 512),
                                     bank(T)), waits=wv_)
                dvv = P.vec("tensor_add", (ff(ob + T * 512, 512),
                                           ff(ob + T * 512, 512),
                                           bank(6 + T)), inc="DV")
            ost_ls[oc % 2] = P.dma("s", y[oc, :, :], ff(ob, 1024),
                                   waits=[("DV", dvv)], sem="ST")
        if debug:
            fin = P.snap()
            wf = [("TE", fin["TE"]), ("DV", fin["DV"]), ("AC", fin["AC"]),
                  ("ST", fin["ST"])]
            P.dma("s", dbg_x1, bf(OFF["S1"], 32768), waits=wf)
            P.dma("s", dbg_s2, bf(OFF["S2"], 32768))
        P.wait("s", [("ST", P.cnt["ST"]), ("LS", P.cnt["LS"])])

        P.cut()
        cut = P.cuts[phases - 100] if phases >= 100 else P.cuts[min(phases, 5) - 1] if phases <= 2 else P.cuts[-(6 - phases)]

        @block.tensor
        def _(t):
            for fn in P.ops["t"][:cut["t"]]:
                fn(t, sems)

        @block.scalar
        def _(a):
            for fn in P.ops["a"][:cut["a"]]:
                fn(a, sems)

        @block.vector
        def _(v):
            for fn in P.ops["v"][:cut["v"]]:
                fn(v, sems)

        @block.gpsimd
        def _(g):
            for fn in P.ops["g"][:cut["g"]]:
                fn(g, sems)

        @block.sync
        def _(s):
            for fn in P.ops["s"][:cut["s"]]:
                fn(s, sems)

    return nc


def _local_idx(j):
    return np.concatenate(
        [np.arange(128 * (4 * t + j), 128 * (4 * t + j) + 128)
         for t in range(8)])


def _host_inputs(L0_attn_hidden_states, L0_attn_Wq, L0_attn_Wk, L0_attn_Wv,
                 L0_attn_Wo, L0_ffn_lora_A, L0_ffn_lora_B, L0_ffn_W,
                 L0_ffn_bias, L0_ffn_lora_weights):
    x = np.asarray(L0_attn_hidden_states)
    C = np.ascontiguousarray
    shared = {
        "wkT": C(np.asarray(L0_attn_Wk).T),
        "wvT": C(np.asarray(L0_attn_Wv).T),
        "wqT": C(np.asarray(L0_attn_Wq).T),
        "woT": C(np.asarray(L0_attn_Wo).T),
        "wT": C(np.asarray(L0_ffn_W).T),
        "aallT": C(np.asarray(L0_ffn_lora_A).reshape(128, D).T),
        "ballT": C(np.asarray(L0_ffn_lora_B).transpose(0, 2, 1)
                   .reshape(128, D)),
        "bias2d": C(np.asarray(L0_ffn_bias).reshape(32, 128).T),
        "w2rep": np.repeat(np.asarray(L0_ffn_lora_weights), 16)
        .reshape(128, 1).astype(np.float32),
    }
    inv = 10000.0 ** (-(np.arange(64, dtype=np.float64)) / 64.0)
    ang = np.arange(S, dtype=np.float64)[None, :] * inv[:, None]
    cosh = np.cos(ang)
    sinh = np.sin(ang)
    cos_feat = np.concatenate([cosh, cosh], 0).astype(np.float32)
    ssgn_feat = np.concatenate([-sinh, sinh], 0).astype(np.float32)
    shared["cosk"] = C(cos_feat)
    shared["ssgnk"] = C(ssgn_feat)

    xT = [C(x[b].T) for b in range(B)]
    in_maps = []
    for c in range(8):
        b, j = c // 4, c % 4
        idx = _local_idx(j)
        m = dict(shared)
        m["xT"] = xT[b]
        m["xlT"] = C(xT[b][:, idx])
        m["cosq"] = C(cos_feat[:, idx])
        m["ssgnq"] = C(ssgn_feat[:, idx])
        pk = np.arange(128)[:, None]
        r = np.arange(512)[None, :]
        qpos = 128 * (4 * (r // 128) + j) + (r % 128)
        mask = np.zeros((128, 16 * 512), dtype=np.float32)
        for kc in range(16):
            mask[:, kc * 512:(kc + 1) * 512] = (128 * kc + pk <= qpos)
        m["maskin"] = mask
        in_maps.append(m)
    return in_maps


_cache = {}


def kernel(**inputs):
    in_maps = _host_inputs(**inputs)
    if "nc" not in _cache:
        _cache["nc"] = build()
    res = run_bass_kernel_spmd(_cache["nc"], in_maps, core_ids=list(range(8)))
    out = np.empty((B, S, D), dtype=np.float32)
    for c in range(8):
        b, j = c // 4, c % 4
        yv = np.asarray(res.results[c]["y"]).reshape(D, LT)
        out[b, _local_idx(j), :] = yv.T
    return out
